# version 01 (final)
# baseline (speedup 1.0000x reference)
import math
from contextlib import ExitStack

import numpy as np
import ml_dtypes

import concourse.bass as bass
import concourse.mybir as mybir
from concourse.bass_utils import run_bass_kernel_spmd

F32 = mybir.dt.float32
BF16 = mybir.dt.bfloat16
AF = mybir.ActivationFunctionType
ALU = mybir.AluOpType
AX = mybir.AxisListType
BF = ml_dtypes.bfloat16

B, T, D = 4, 4096, 1024
H = 2048
NEXP = 10
NHEAD, DH = 16, 64
TPC = 2048
NTAU = 4
DC, HC = 8, 16
ROPE_BASE = 10000.0
EPS = 1e-5
VSTR = 66
VROWS = NHEAD * VSTR * TPC // TPC
CCROWS = 1024 + 1056 + 1
VBASE = 1024 * TPC
SBASE = 2080 * TPC

DEV_PHASES = ("ffn1", "attn", "ffn2")
DEV_NEXP = NEXP
DEV_TAPS = ()

ENGS = ("sync", "gpsimd", "tensor", "vector", "scalar")


def chunks_for_rank(r):
    keep = (0, 3) if r == 0 else (1, 2)
    return [c for c in range(32) if c % 4 in keep]


def tokens_for_rank(r):
    out = []
    for c in chunks_for_rank(r):
        out.extend(range(c * 128, (c + 1) * 128))
    return np.array(out, dtype=np.int64)


def owner_and_local(g):
    r = 0 if g % 4 in (0, 3) else 1
    return r, chunks_for_rank(r).index(g)



def host_tables():
    t = {}
    S = np.zeros((128, 128), dtype=np.float32)
    for i in range(64):
        S[2 * i + 1, 2 * i] = -1.0
        S[2 * i, 2 * i + 1] = 1.0
    t["swapneg"] = S.astype(BF)
    t["ident_f32"] = np.eye(128, dtype=np.float32)
    t["ones_div"] = np.full((128, 1), 1.0 / D, dtype=np.float32)
    t["ones_row"] = np.ones((1, 128), dtype=np.float32)
    ind = np.zeros((NEXP, NEXP * 128), dtype=np.float32)
    for n in range(NEXP):
        ind[n, n * 128:(n + 1) * 128] = 1.0
    t["exp_ind"] = ind.astype(BF)
    hind = np.zeros((128, NHEAD * NHEAD), dtype=np.float32)
    for h in range(NHEAD):
        hind[:, h * NHEAD + h] = 1.0
    t["head_ind"] = hind.astype(BF)
    qn = np.zeros((128, 8 * 16), dtype=np.float32)
    for c in range(8):
        qn[0:64, c * 16 + 2 * c] = 1.0
        qn[64:128, c * 16 + 2 * c + 1] = 1.0
    t["qn_ind"] = qn.astype(BF)
    ep = np.zeros((16, 8 * 128), dtype=np.float32)
    for pr in range(8):
        ep[2 * pr, pr * 128:pr * 128 + 64] = 1.0
        ep[2 * pr + 1, pr * 128 + 64:pr * 128 + 128] = 1.0
    t["pair_ind"] = ep
    return t


def host_percore(x, core):
    rank, batch = core & 1, core >> 1
    toks = tokens_for_rank(rank)
    xs = np.asarray(x[batch]).reshape(T, D)[toks]
    x_fm = np.ascontiguousarray(xs.T.astype(np.float32))
    inv = 1.0 / ROPE_BASE ** (np.arange(0, DH, 2, dtype=np.float64) / DH)
    pos = toks.astype(np.float64)
    pair = (np.arange(128) % DH) // 2
    ang = pos[None, :] * inv[pair][:, None]
    cos_fm = np.cos(ang).astype(np.float32).astype(BF)
    sin_fm = np.sin(ang).astype(np.float32).astype(BF)
    masks = np.zeros((NTAU, 8, 128, 512), dtype=BF)
    for tau in range(NTAU):
        qpos = toks[tau * 512:(tau + 1) * 512]
        for j in range(8):
            g = 8 * tau + j
            kpos = np.arange(g * 128, (g + 1) * 128)
            masks[tau, j] = (kpos[:, None] <= qpos[None, :]).astype(BF)
    return x_fm, cos_fm, sin_fm, masks.reshape(NTAU * 8 * 128, 512)


def host_weights(inp):
    w = {}
    w["wqkv"] = np.ascontiguousarray(np.asarray(inp["L0_attn_Wqkv"], dtype=np.float32))
    w["wproj"] = np.ascontiguousarray(np.asarray(inp["L0_attn_Wproj"], dtype=np.float32))
    w["w1"] = np.ascontiguousarray(np.concatenate(
        [np.asarray(inp["L0_ffn_rw1"]), np.asarray(inp["L0_ffn_sw1"])], axis=0).astype(np.float32))
    w["w2"] = np.ascontiguousarray(np.concatenate(
        [np.asarray(inp["L0_ffn_rw2"]), np.asarray(inp["L0_ffn_sw2"])], axis=0).astype(np.float32))
    w["w3"] = np.ascontiguousarray(np.concatenate(
        [np.asarray(inp["L0_ffn_rw3"]), np.asarray(inp["L0_ffn_sw3"])], axis=0).astype(np.float32))
    b1 = np.concatenate([np.asarray(inp["L0_ffn_rb1"]), np.asarray(inp["L0_ffn_sb1"])], axis=0)
    b2 = np.concatenate([np.asarray(inp["L0_ffn_rb2"]), np.asarray(inp["L0_ffn_sb2"])], axis=0)
    b3 = np.concatenate([np.asarray(inp["L0_ffn_rb3"]), np.asarray(inp["L0_ffn_sb3"])], axis=0)

    def colmajor(b):
        a = np.zeros((128, NEXP * HC), dtype=np.float32)
        for n in range(NEXP):
            for hc in range(HC):
                a[:, n * HC + hc] = b[n, hc * 128:(hc + 1) * 128]
        return a
    w["b1c"] = colmajor(b1.astype(np.float32))
    w["b2c"] = colmajor(b2.astype(np.float32))
    w["b3"] = np.ascontiguousarray(b3.astype(np.float32))
    gw = np.asarray(inp["L0_ffn_gate_w"], dtype=np.float32)
    ga = np.zeros((128, DC * 8), dtype=np.float32)
    for c in range(DC):
        ga[:, c * 8:(c + 1) * 8] = gw[c * 128:(c + 1) * 128, :]
    w["gatew"] = ga
    w["gateb"] = np.asarray(inp["L0_ffn_gate_b"], dtype=np.float32).reshape(8, 1)
    return w



def ap_custom(handle, offset, dims):
    return bass.AP(handle, offset, [list(d) for d in dims])


class Prog:

    def __init__(self, nc):
        self.nc = nc
        self.q = {e: [] for e in ENGS}
        self.sem = {}
        self.cnt = {}
        for s in ("sd", "gd", "pe", "ve", "se", "cc"):
            self.sem[s] = nc.alloc_semaphore(name="s_" + s)
            self.cnt[s] = 0

    def wait(self, eng, sem, val):
        if val > 0:
            s = self.sem[sem]
            self.q[eng].append(lambda e, s=s, val=val: e.wait_ge(s, val))

    def op(self, eng, fn, inc=None, by=1):
        if inc is None:
            self.q[eng].append(lambda e, fn=fn: fn(e))
            return self.cnt
        s = self.sem[inc]
        self.q[eng].append(lambda e, fn=fn, s=s, by=by: fn(e).then_inc(s, by))
        self.cnt[inc] += by
        return self.cnt[inc]

    def dma(self, eng, out, in_, sem):
        return self.op(eng, lambda e, out=out, in_=in_: e.dma_start(out=out, in_=in_),
                       inc=sem, by=16)

    def replay(self, blk):
        for name, dec in (("sync", blk.sync), ("gpsimd", blk.gpsimd),
                          ("tensor", blk.tensor), ("vector", blk.vector),
                          ("scalar", blk.scalar)):
            queue = self.q[name]

            def body(e, queue=queue):
                for fn in queue:
                    fn(e)
            dec(body)



def build_program(phases=DEV_PHASES, nexp=DEV_NEXP, taps=DEV_TAPS, exp_stage="full", attn_norm=True, detect_races=True):
    nc = bass.Bass("TRN2", target_bir_lowering=False, debug=False,
                   detect_race_conditions=detect_races)

    dx = nc.dram_tensor("x_fm", [D, TPC], F32, kind="ExternalInput").ap()
    dout = nc.dram_tensor("out_fm", [D, TPC], F32, kind="ExternalOutput").ap()
    dcos = nc.dram_tensor("cos_fm", [128, TPC], BF16, kind="ExternalInput").ap()
    dsin = nc.dram_tensor("sin_fm", [128, TPC], BF16, kind="ExternalInput").ap()
    dmask = nc.dram_tensor("masks", [NTAU * 8 * 128, 512], BF16, kind="ExternalInput").ap()

    dwqkv = nc.dram_tensor("wqkv", [D, 3 * D], F32, kind="ExternalInput").ap()
    dwproj = nc.dram_tensor("wproj", [D, D], F32, kind="ExternalInput").ap()
    dw1 = nc.dram_tensor("w1", [NEXP, D, H], F32, kind="ExternalInput").ap()
    dw2 = nc.dram_tensor("w2", [NEXP, D, H], F32, kind="ExternalInput").ap()
    dw3 = nc.dram_tensor("w3", [NEXP, H, D], F32, kind="ExternalInput").ap()
    db1 = nc.dram_tensor("b1c", [128, NEXP * HC], F32, kind="ExternalInput").ap()
    db2 = nc.dram_tensor("b2c", [128, NEXP * HC], F32, kind="ExternalInput").ap()
    db3 = nc.dram_tensor("b3", [NEXP, D], F32, kind="ExternalInput").ap()
    dgw = nc.dram_tensor("gatew", [128, DC * 8], F32, kind="ExternalInput").ap()
    dgb = nc.dram_tensor("gateb", [8, 1], F32, kind="ExternalInput").ap()

    dswap = nc.dram_tensor("swapneg", [128, 128], BF16, kind="ExternalInput").ap()
    didentf = nc.dram_tensor("ident_f32", [128, 128], F32, kind="ExternalInput").ap()
    dodiv = nc.dram_tensor("ones_div", [128, 1], F32, kind="ExternalInput").ap()
    dorow = nc.dram_tensor("ones_row", [1, 128], F32, kind="ExternalInput").ap()
    dexpind = nc.dram_tensor("exp_ind", [NEXP, NEXP * 128], BF16, kind="ExternalInput").ap()
    dheadind = nc.dram_tensor("head_ind", [128, 16 * 16], BF16, kind="ExternalInput").ap()
    dqnind = nc.dram_tensor("qn_ind", [128, 8 * 16], BF16, kind="ExternalInput").ap()
    dpairind = nc.dram_tensor("pair_ind", [16, 8 * 128], F32, kind="ExternalInput").ap()

    cc_in = nc.dram_tensor("cc_in", [CCROWS, TPC], BF16, kind="Internal")
    cc_out = nc.dram_tensor("cc_out", [2 * CCROWS, TPC], BF16, kind="Internal")
    dnegb = nc.dram_tensor("negb_sc", [1, 16], F32, kind="Internal").ap()

    tap_aps = {}
    for tname, shape in (("z1", [D, TPC]), ("gate1", [NEXP, NTAU * 512]),
                         ("x1", [D, TPC]), ("q", [D, TPC]), ("x2", [D, TPC]),
                         ("glu", [128, HC * TPC]), ("wbc", [128, NTAU * 512]),
                         ("negb", [128, 16]), ("y", [128, 8 * TPC]),
                         ("kv", [2 * CCROWS, TPC]), ("sums", [16, 512]),
                         ("aT", [128, 2 * 512])):
        if tname in taps:
            tap_aps[tname] = nc.dram_tensor("tap_" + tname, shape, F32,
                                            kind="ExternalOutput").ap()

    p = Prog(nc)
    ctx = ExitStack()
    sb = lambda name, shape, dt: ctx.enter_context(nc.sbuf_tensor(name, shape, dt))

    x_sb = sb("x_sb", [128, DC * TPC], F32)
    eps_sb = sb("eps_sb", [128, 1], F32)
    identf_sb = sb("identf_sb", [128, 128], F32)
    odiv_sb = sb("odiv_sb", [128, 1], F32)
    orow_sb = sb("orow_sb", [1, 128], F32)
    expind_sb = sb("expind_sb", [NEXP, NEXP * 128], BF16)
    b1_sb = sb("b1_sb", [128, NEXP * HC], F32)
    b2_sb = sb("b2_sb", [128, NEXP * HC], F32)
    b3_sb = sb("b3_sb", [NEXP, D], BF16)
    gw_sb = sb("gw_sb", [128, DC * 8], F32)
    gb_sb = sb("gb_sb", [8, 1], F32)

    p.dma("sync", x_sb.ap().rearrange("p (c t) -> p c t", c=DC),
          dx.rearrange("(c p) t -> p c t", p=128), "sd")
    for dst, src in ((identf_sb, didentf), (odiv_sb, dodiv), (orow_sb, dorow),
                     (expind_sb, dexpind),
                     (b1_sb, db1), (b2_sb, db2), (gw_sb, dgw), (gb_sb, dgb)):
        p.dma("sync", dst[:], src[:, :], "sd")
    p.dma("gpsimd", b3_sb[:], db3[:, :], "gd")
    p.op("vector", lambda e: e.memset(eps_sb[:], EPS))
    loads_sd, loads_gd = p.cnt["sd"], p.cnt["gd"]

    entry = dict(pe=0, ve=0, se=0, sd=loads_sd, gd=loads_gd)

    def emit_ffn(tag, tap_z=None, tap_gate=None, tap_x=None):
        stage = exp_stage
        fctx = ExitStack()
        fsb = lambda name, shape, dt: fctx.enter_context(
            nc.sbuf_tensor(tag + name, shape, dt))
        z_sb = fsb("z", [128, DC * TPC], BF16)
        glu_sb = fsb("glu", [128, HC * TPC], BF16)
        w12_sb = fsb("w12", [128, 2 * 2 * DC * 128], BF16)
        w3s_sb = fsb("w3s", [128, 2 * HC * 128], BF16)
        sil_sb = fsb("sil", [128, NTAU * 512], F32)
        wbc_sb = fsb("wbc", [128, NTAU * 512], BF16)
        gfm_sb = fsb("gfm", [NEXP, NTAU * 512], BF16)
        zf_sb = fsb("zf", [128, 2 * 512], F32)
        sq_sb = zf_sb
        mrg_sb = fsb("mr", [8, 2 * 512], F32)
        mr_sb = mrg_sb[0:1, :]
        gl_sb = mrg_sb[0:8, 0:512]
        gtm_sb = fsb("gtm", [128, 16], F32)
        grow_sb = fsb("grow", [128, 4], F32)

        w12v = w12_sb.ap().rearrange("p (s m c k) -> p s m c k", s=2, m=2, c=DC)
        w3v = w3s_sb.ap().rearrange("p (s h k) -> p s h k", s=2, h=HC)

        pctx = ExitStack()
        pp = lambda name, shape: pctx.enter_context(nc.psum_tensor(tag + name, shape, F32))
        p_mean = pp("pmean", [1, 512])
        p_ex2 = pp("pex2", [1, 512])
        p_bc = pp("pbc", [128, 512])
        p_bc2 = pp("pbc2", [128, 512])
        p_g = pp("pg", [8, 512])
        p_tp1 = pp("ptp1", [128, 8])
        p_tp2 = pp("ptp2", [8, 128])

        for e in ("vector", "tensor", "scalar"):
            p.wait(e, "sd", loads_sd)
            p.wait(e, "pe", entry["pe"])
            p.wait(e, "ve", entry["ve"])
            p.wait(e, "se", entry["se"])
        gfm_ready = p.op("vector", lambda e: e.memset(gfm_sb[:], 0.25), inc="ve")
        if tag == "f2_":
            p.wait("gpsimd", "pe", entry["pe"])
            p.wait("gpsimd", "ve", entry["ve"])
            p.wait("gpsimd", "se", entry["se"])

        zfdone_prev = None
        last_tp_pe = 0
        prev_rows_ve = 0
        for tau in range(NTAU):
            tc = tau * 512
            sqdone = {}
            if prev_rows_ve:
                p.wait("tensor", "ve", prev_rows_ve)
            for c in range(DC):
                sl = (c % 2) * 512
                if c >= 2:
                    p.wait("vector", "pe", sqdone[c - 2])
                elif zfdone_prev is not None:
                    pe_v, se_v = zfdone_prev[DC - 2 + c]
                    p.wait("vector", "pe", pe_v)
                    p.wait("vector", "se", se_v)
                sq_ve = p.op("vector", lambda e, c=c, tc=tc, sl=sl: e.tensor_mul(
                    sq_sb[:, sl:sl + 512],
                    x_sb[:, c * TPC + tc:c * TPC + tc + 512],
                    x_sb[:, c * TPC + tc:c * TPC + tc + 512]), inc="ve")
                p.op("tensor", lambda e, c=c, tc=tc: e.matmul(
                    p_mean[:], odiv_sb[:, :],
                    x_sb[:, c * TPC + tc:c * TPC + tc + 512],
                    start=(c == 0), stop=(c == DC - 1), skip_group_check=True))
                p.wait("tensor", "ve", sq_ve)
                sqdone[c] = p.op("tensor", lambda e, c=c, sl=sl: e.matmul(
                    p_ex2[:], odiv_sb[:, :], sq_sb[:, sl:sl + 512],
                    start=(c == 0), stop=(c == DC - 1), skip_group_check=True), inc="pe")
            stats_pe = p.cnt["pe"]

            p.wait("vector", "pe", stats_pe)
            p.wait("vector", "pe", last_tp_pe)
            c1 = p.op("vector", lambda e: e.tensor_copy(mrg_sb[0:1, 0:512], p_mean[:]),
                      inc="ve")
            p.wait("vector", "ve", c1)
            c2 = p.op("vector", lambda e: e.tensor_mul(
                mrg_sb[0:1, 512:1024], mrg_sb[0:1, 0:512], mrg_sb[0:1, 0:512]),
                inc="ve")
            p.wait("vector", "ve", c2)
            rows_ve = p.op("vector", lambda e: e.tensor_sub(
                mrg_sb[0:1, 512:1024], p_ex2[:], mrg_sb[0:1, 512:1024]), inc="ve")
            prev_rows_ve = rows_ve
            p.wait("scalar", "ve", rows_ve)
            l1 = p.op("scalar", lambda e: e.activation(
                mrg_sb[0:1, 512:1024], mrg_sb[0:1, 512:1024],
                AF.Ln, bias=eps_sb[0:1, 0:1]), inc="se")
            p.wait("scalar", "se", l1)
            rows_se = p.op("scalar", lambda e: e.activation(
                mrg_sb[0:1, 512:1024], mrg_sb[0:1, 512:1024], AF.Exp, scale=-0.5), inc="se")

            p.wait("tensor", "se", rows_se)
            p.op("tensor", lambda e: e.matmul(p_bc[:], orow_sb[:, :], mrg_sb[0:1, 0:512],
                                              start=True, stop=True))
            bc_pe = p.op("tensor", lambda e: e.matmul(
                p_bc2[:], orow_sb[:, :], mrg_sb[0:1, 512:1024],
                start=True, stop=True), inc="pe")

            zfdone = {}
            for c in range(DC):
                sl = (c % 2) * 512
                if c == 0:
                    p.wait("vector", "pe", bc_pe)
                if c >= 2:
                    pe_v, se_v = zfdone[c - 2]
                    p.wait("vector", "pe", pe_v)
                    p.wait("vector", "se", se_v)
                s1 = p.op("vector", lambda e, c=c, tc=tc, sl=sl: e.tensor_sub(
                    zf_sb[:, sl:sl + 512],
                    x_sb[:, c * TPC + tc:c * TPC + tc + 512], p_bc[:]), inc="ve")
                p.wait("vector", "ve", s1)
                zf_ve = p.op("vector", lambda e, sl=sl: e.tensor_mul(
                    zf_sb[:, sl:sl + 512], zf_sb[:, sl:sl + 512], p_bc2[:]), inc="ve")
                p.wait("scalar", "ve", zf_ve)
                zc_se = p.op("scalar", lambda e, c=c, tc=tc, sl=sl: e.copy(
                    z_sb[:, c * TPC + tc:c * TPC + tc + 512], zf_sb[:, sl:sl + 512]),
                    inc="se")
                p.wait("tensor", "ve", zf_ve)
                zc_pe = p.op("tensor", lambda e, c=c, sl=sl: e.matmul(
                    p_g[:], gw_sb[:, c * 8:(c + 1) * 8], zf_sb[:, sl:sl + 512],
                    start=(c == 0), stop=(c == DC - 1), skip_group_check=True), inc="pe")
                zfdone[c] = (zc_pe, zc_se)
            gate_pe = p.cnt["pe"]

            p.wait("vector", "pe", gate_pe)
            gl_ve = p.op("vector", lambda e: e.tensor_single_scalar(
                gl_sb, p_g[:], gb_sb[:, :], ALU.add), inc="ve")

            for sub in range(4):
                p.wait("tensor", "ve", gl_ve)
                if sub > 0:
                    p.wait("tensor", "ve", m_ve)
                tp_pe = p.op("tensor", lambda e, sub=sub: e.transpose(
                    p_tp1[:, :], mrg_sb[0:8, sub * 128:(sub + 1) * 128],
                    identf_sb[0:8, 0:8]), inc="pe")
                p.wait("vector", "pe", tp_pe)
                c = p.op("vector", lambda e: e.tensor_copy(gtm_sb[:, 0:8], p_tp1[:, :]),
                         inc="ve")
                p.wait("vector", "ve", c)
                c = p.op("vector", lambda e: e.reduce_max(grow_sb[:, 0:1],
                                                          gtm_sb[:, 0:8], axis=AX.X),
                         inc="ve")
                p.wait("vector", "ve", c)
                m_ve = p.op("vector", lambda e: e.tensor_scalar_mul(
                    grow_sb[:, 1:2], grow_sb[:, 0:1], -1.0), inc="ve")
                p.wait("scalar", "ve", m_ve)
                e_se = p.op("scalar", lambda e: e.activation(
                    gtm_sb[:, 8:16], gtm_sb[:, 0:8], AF.Exp, bias=grow_sb[:, 1:2]),
                    inc="se")
                p.wait("vector", "se", e_se)

                def vstep(fn):
                    c = p.op("vector", fn, inc="ve")
                    p.wait("vector", "ve", c)
                    return c
                vstep(lambda e: e.reduce_sum(grow_sb[:, 0:1], gtm_sb[:, 8:16],
                                             axis=AX.X))
                vstep(lambda e: e.reciprocal(grow_sb[:, 1:2], grow_sb[:, 0:1]))
                vstep(lambda e: e.tensor_single_scalar(
                    gtm_sb[:, 0:8], gtm_sb[:, 8:16], grow_sb[:, 1:2], ALU.mult))
                vstep(lambda e: e.reduce_max(grow_sb[:, 2:3], gtm_sb[:, 0:8],
                                             axis=AX.X))
                vstep(lambda e: e.tensor_single_scalar(
                    gtm_sb[:, 8:16], gtm_sb[:, 0:8], grow_sb[:, 2:3], ALU.is_ge))
                vstep(lambda e: e.scalar_tensor_tensor(
                    gtm_sb[:, 8:16], gtm_sb[:, 8:16], -1e9, gtm_sb[:, 0:8],
                    ALU.mult, ALU.add))
                vstep(lambda e: e.reduce_max(grow_sb[:, 3:4], gtm_sb[:, 8:16],
                                             axis=AX.X))
                vstep(lambda e: e.tensor_single_scalar(
                    gtm_sb[:, 8:16], gtm_sb[:, 0:8], grow_sb[:, 3:4], ALU.is_ge))
                w_ve = vstep(lambda e: e.tensor_mul(
                    gtm_sb[:, 0:8], gtm_sb[:, 0:8], gtm_sb[:, 8:16]))
                p.wait("tensor", "ve", w_ve)
                if sub > 0:
                    p.wait("tensor", "se", gsub_se)
                tpb_pe = p.op("tensor", lambda e: e.transpose(
                    p_tp2[:, :], gtm_sb[:, 0:8], identf_sb[:, :]), inc="pe")
                p.wait("scalar", "pe", tpb_pe)
                p.wait("scalar", "ve", gfm_ready)
                gsub_se = p.op("scalar", lambda e, tc=tc, sub=sub: e.activation(
                    gfm_sb[0:8, tc + sub * 128:tc + (sub + 1) * 128], p_tp2[:, :],
                    AF.Copy, scale=0.5), inc="se")
                last_tp_pe = tp_pe
            zfdone_prev = zfdone
        ln_pe, ln_ve, ln_se = p.cnt["pe"], p.cnt["ve"], p.cnt["se"]

        if tap_z is not None:
            p.wait("gpsimd", "se", ln_se)
            p.dma("gpsimd", tap_z.rearrange("(c p) t -> p c t", p=128),
                  z_sb.ap().rearrange("p (c t) -> p c t", c=DC), "gd")
        if tap_gate is not None:
            p.wait("gpsimd", "se", ln_se)
            p.dma("gpsimd", tap_gate, gfm_sb[:], "gd")

        pctx.close()

        ectx = ExitStack()
        ep = lambda name, shape: ectx.enter_context(nc.psum_tensor(tag + name, shape, F32))
        p_ps = [ep(f"pp_{i}", [128, 512]) for i in range(8)]

        se_silu = {}
        ve_glu = {}
        ve_add = {}
        pe_p1 = {}
        pe_p2 = {}
        pe_mm3 = {}
        se_wbc = {}
        gd_w12 = {}
        gd_w3 = {}

        first_mm = True

        def issue_w12(n, h):
            key = (n, h - 2) if h >= 2 else ((n - 1, HC - 2 + h) if n >= 1 else None)
            if key is not None:
                p.wait("gpsimd", "pe", pe_p2[key])
            p.dma("gpsimd", w12v[:, h % 2, 0, :, :],
                  dw1[n, :, h * 128:(h + 1) * 128].rearrange("(c p) k -> p c k", p=128),
                  "gd")
            gd_w12[(n, h)] = p.dma(
                "gpsimd", w12v[:, h % 2, 1, :, :],
                dw2[n, :, h * 128:(h + 1) * 128].rearrange("(c p) k -> p c k", p=128),
                "gd")

        def issue_w3(n, d):
            key = (n, d - 2) if d >= 2 else ((n - 1, DC - 2 + d) if n >= 1 else None)
            if key is not None:
                p.wait("gpsimd", "pe", pe_mm3[key])
            gd_w3[(n, d)] = p.dma(
                "gpsimd", w3v[:, d % 2, :, :],
                dw3[n, :, d * 128:(d + 1) * 128].rearrange("(h p) k -> p h k", p=128),
                "gd")

        for n in range(nexp):
            issue_w12(n, 0)
            issue_w12(n, 1)

            if stage == "dma":
                continue
            if first_mm:
                p.wait("tensor", "se", ln_se)
                p.wait("tensor", "ve", ln_ve)
                first_mm = False
            if n >= 1:
                p.wait("tensor", "ve", ve_add[(n - 1, DC - 1)])
            wbc_pe = 0
            for tau in range(NTAU):
                wbc_pe = p.op("tensor", lambda e, n=n, tau=tau: e.matmul(
                    p_ps[tau][:], expind_sb[:, n * 128:(n + 1) * 128],
                    gfm_sb[:, tau * 512:(tau + 1) * 512],
                    start=True, stop=True, skip_group_check=True), inc="pe")
            p.wait("scalar", "pe", wbc_pe)
            if n >= 1:
                p.wait("scalar", "ve", ve_glu[(n - 1, HC - 1)])
            wbc_se = 0
            for tau in range(NTAU):
                wbc_se = p.op("scalar", lambda e, tau=tau: e.copy(
                    wbc_sb[:, tau * 512:(tau + 1) * 512], p_ps[tau][:]), inc="se")
            se_wbc[n] = wbc_se

            if stage == "wbc":
                continue
            for h in range(HC):
                p.wait("tensor", "gd", gd_w12[(n, h)])
                if h == 0:
                    p.wait("tensor", "se", se_wbc[n])
                else:
                    p.wait("tensor", "se", se_silu[(n, h - 1)])
                for d in range(DC):
                    for tau in range(NTAU):
                        last = (d == DC - 1 and tau == NTAU - 1)
                        p.op("tensor", lambda e, n=n, h=h, d=d, tau=tau: e.matmul(
                            p_ps[tau][:], w12v[:, h % 2, 0, d, :],
                            z_sb[:, d * TPC + tau * 512:d * TPC + tau * 512 + 512],
                            start=(d == 0), stop=(d == DC - 1), skip_group_check=True),
                            inc="pe" if last else None)
                pe_p1[(n, h)] = p.cnt["pe"]
                if h == 0:
                    if n >= 1:
                        p.wait("tensor", "ve", ve_add[(n - 1, DC - 1)])
                else:
                    p.wait("tensor", "ve", ve_glu[(n, h - 1)])
                for d in range(DC):
                    for tau in range(NTAU):
                        last = (d == DC - 1 and tau == NTAU - 1)
                        p.op("tensor", lambda e, n=n, h=h, d=d, tau=tau: e.matmul(
                            p_ps[4 + tau][:], w12v[:, h % 2, 1, d, :],
                            z_sb[:, d * TPC + tau * 512:d * TPC + tau * 512 + 512],
                            start=(d == 0), stop=(d == DC - 1), skip_group_check=True),
                            inc="pe" if last else None)
                pe_p2[(n, h)] = p.cnt["pe"]
                if h + 2 < HC:
                    issue_w12(n, h + 2)
                elif stage == "full" and h == HC - 1:
                    issue_w3(n, 0)
                    issue_w3(n, 1)

                if stage == "mm_only":
                    se_silu[(n, h)] = 0
                    ve_glu[(n, h)] = 0
                    continue
                p.wait("scalar", "pe", pe_p1[(n, h)])
                if h >= 1:
                    p.wait("scalar", "ve", ve_glu[(n, h - 1)])
                elif n >= 1:
                    p.wait("scalar", "ve", ve_glu[(n - 1, HC - 1)])
                bc = n * HC + h
                for tau in range(NTAU):
                    i = p.op("scalar", lambda e, tau=tau, bc=bc: e.activation(
                        sil_sb[:, tau * 512:(tau + 1) * 512], p_ps[tau][:],
                        AF.Silu, bias=b1_sb[:, bc:bc + 1]),
                        inc="se" if tau == NTAU - 1 else None)
                se_silu[(n, h)] = p.cnt["se"]

                if stage == "silu_only":
                    ve_glu[(n, h)] = 0
                    continue
                p.wait("vector", "se", se_silu[(n, h)])
                p.wait("vector", "se", se_wbc[n])
                p.wait("vector", "pe", pe_p2[(n, h)])
                if n >= 1 and h == 0:
                    p.wait("vector", "pe", pe_mm3[(n - 1, DC - 1)])
                for tau in range(NTAU):
                    mu = p.op("vector", lambda e, tau=tau: e.tensor_mul(
                        sil_sb[:, tau * 512:(tau + 1) * 512],
                        sil_sb[:, tau * 512:(tau + 1) * 512],
                        wbc_sb[:, tau * 512:(tau + 1) * 512]),
                        inc="ve" if tau == NTAU - 1 else None)
                p.wait("vector", "ve", mu)
                for tau in range(NTAU):
                    p.op("vector", lambda e, tau=tau, h=h, bc=bc: e.scalar_tensor_tensor(
                        glu_sb[:, h * TPC + tau * 512:h * TPC + tau * 512 + 512],
                        p_ps[4 + tau][:], b2_sb[:, bc:bc + 1],
                        sil_sb[:, tau * 512:(tau + 1) * 512], ALU.add, ALU.mult),
                        inc="ve" if tau == NTAU - 1 else None)
                ve_glu[(n, h)] = p.cnt["ve"]

            if stage in ("glu", "mm_only", "silu_only"):
                continue
            is_last = (n == nexp - 1)
            for d in range(DC):
                p.wait("tensor", "gd", gd_w3[(n, d)])
                p.wait("tensor", "ve", ve_glu[(n, HC - 1)])
                sl = (d % 2) * 4
                if d == 0:
                    p.wait("tensor", "se", se_silu[(n, HC - 1)])
                elif d == 1:
                    pass
                else:
                    p.wait("tensor", "ve", ve_add[(n, d - 2)])
                for h in range(HC):
                    stop = (h == HC - 1) and not is_last
                    for tau in range(NTAU):
                        lastmm = (h == HC - 1 and tau == NTAU - 1 and not is_last)
                        p.op("tensor", lambda e, d=d, h=h, tau=tau, sl=sl, stop=stop:
                             e.matmul(
                                 p_ps[sl + tau][:], w3v[:, d % 2, h, :],
                                 glu_sb[:, h * TPC + tau * 512:h * TPC + tau * 512 + 512],
                                 start=(h == 0), stop=stop, skip_group_check=True),
                             inc="pe" if lastmm else None)
                if is_last:
                    for tau in range(NTAU):
                        p.op("tensor", lambda e, d=d, tau=tau, sl=sl: e.matmul(
                            p_ps[sl + tau][:], b3_sb[:, d * 128:(d + 1) * 128],
                            gfm_sb[:, tau * 512:(tau + 1) * 512],
                            start=False, stop=True, skip_group_check=True),
                            inc="pe" if tau == NTAU - 1 else None)
                pe_mm3[(n, d)] = p.cnt["pe"]
                if d + 2 < DC:
                    issue_w3(n, d + 2)

                p.wait("vector", "pe", pe_mm3[(n, d)])
                for tau in range(NTAU):
                    p.op("vector", lambda e, d=d, tau=tau, sl=sl: e.tensor_add(
                        x_sb[:, d * TPC + tau * 512:d * TPC + tau * 512 + 512],
                        x_sb[:, d * TPC + tau * 512:d * TPC + tau * 512 + 512],
                        p_ps[sl + tau][:]),
                        inc="ve" if tau == NTAU - 1 else None)
                ve_add[(n, d)] = p.cnt["ve"]

        if "glu" in taps and tag == "f1_":
            p.wait("gpsimd", "ve", p.cnt["ve"])
            p.dma("gpsimd", tap_aps["glu"], glu_sb[:], "gd")
        if "wbc" in taps and tag == "f1_":
            p.wait("gpsimd", "ve", p.cnt["ve"])
            p.wait("gpsimd", "se", p.cnt["se"])
            p.dma("gpsimd", tap_aps["wbc"], wbc_sb[:], "gd")
        fctx.close()
        ectx.close()
        out = dict(pe=p.cnt["pe"], ve=p.cnt["ve"], se=p.cnt["se"],
                   sd=p.cnt["sd"], gd=p.cnt["gd"])
        if tap_x is not None:
            p.wait("sync", "ve", out["ve"])
            p.dma("sync", tap_x.rearrange("(c p) t -> p c t", p=128),
                  x_sb.ap().rearrange("p (c t) -> p c t", c=DC), "sd")
            out["sd"] = p.cnt["sd"]
        return out

    def emit_attn(entry):
        actx = ExitStack()
        asb = lambda name, shape, dt: actx.enter_context(
            nc.sbuf_tensor("a_" + name, shape, dt))
        q_sb = asb("q", [128, 8 * TPC], BF16)
        mq_sb = asb("mq", [16, 4], F32)
        negb_sb = asb("negb", [128, 16], F32)

        pactx = ExitStack()
        pasb = lambda name, shape, dt: pactx.enter_context(
            nc.sbuf_tensor("aa_" + name, shape, dt))
        z2_sb = pasb("z2", [128, DC * TPC], BF16)
        cos_sb = pasb("cos", [128, TPC], BF16)
        sin_sb = pasb("sin", [128, TPC], BF16)
        swap_sb = pasb("swap", [128, 128], BF16)
        qnind_sb = pasb("qnind", [128, 8 * 16], BF16)
        wq_sb = pasb("wq", [128, 2 * DC * 128], BF16)
        wv_sb = pasb("wv", [128, DC * 1024], BF16)
        qraw_sb = pasb("qraw", [128, 2 * 2 * 512], BF16)
        t1_sb = pasb("t1", [128, 2 * 512], F32)
        t2_sb = pasb("t2", [128, 2 * 512], F32)
        kf_sb = pasb("kf", [128, 2 * 2 * 512], BF16)
        sqa_sb = pasb("sqa", [128, 2 * 512], BF16)
        vt_sb = pasb("vt", [128, 2 * 1056], BF16)
        zf2_sb = pasb("zf2", [128, 2 * 512], F32)
        mr2_sb = pasb("mr2", [1, 2 * 512], F32)

        p.wait("sync", "ve", entry["ve"])
        p.wait("sync", "se", entry["se"])
        p.dma("sync", cos_sb[:], dcos[:, :], "sd")
        p.dma("sync", sin_sb[:], dsin[:, :], "sd")
        p.dma("sync", swap_sb[:], dswap[:, :], "sd")
        p.dma("sync", qnind_sb[:], dqnind[:, :], "sd")
        aload_sd = p.cnt["sd"]
        p.wait("gpsimd", "ve", entry["ve"])
        p.wait("gpsimd", "se", entry["se"])
        for dd in range(DC):
            p.dma("gpsimd", wv_sb.ap().rearrange("p (c k) -> p c k", c=DC)[:, dd, :],
                  dwqkv[dd * 128:(dd + 1) * 128, 2048:3072], "gd")
        wv_gd = p.cnt["gd"]

        pctx = ExitStack()
        pp = lambda name, shape: pctx.enter_context(
            nc.psum_tensor("ap_" + name, shape, F32))
        p_mean = pp("mean", [1, 512])
        p_ex2 = pp("ex2", [1, 512])
        p_bc = pp("bc", [128, 512])
        p_bc2 = pp("bc2", [128, 512])
        for e in ("vector", "tensor", "scalar"):
            p.wait(e, "pe", entry["pe"])
            p.wait(e, "ve", entry["ve"])
            p.wait(e, "se", entry["se"])
        prev_rows_ve = 0
        zfdone_prev = None
        for tau in range(NTAU):
            tc = tau * 512
            sqdone = {}
            if prev_rows_ve:
                p.wait("tensor", "ve", prev_rows_ve)
            for c in range(DC):
                sl = (c % 2) * 512
                if c >= 2:
                    p.wait("vector", "pe", sqdone[c - 2])
                elif zfdone_prev is not None:
                    pe_v, se_v = zfdone_prev[DC - 2 + c]
                    p.wait("vector", "pe", pe_v)
                    p.wait("vector", "se", se_v)
                sq_ve = p.op("vector", lambda e, c=c, tc=tc, sl=sl: e.tensor_mul(
                    zf2_sb[:, sl:sl + 512],
                    x_sb[:, c * TPC + tc:c * TPC + tc + 512],
                    x_sb[:, c * TPC + tc:c * TPC + tc + 512]), inc="ve")
                p.op("tensor", lambda e, c=c, tc=tc: e.matmul(
                    p_mean[:], odiv_sb[:, :],
                    x_sb[:, c * TPC + tc:c * TPC + tc + 512],
                    start=(c == 0), stop=(c == DC - 1), skip_group_check=True))
                p.wait("tensor", "ve", sq_ve)
                sqdone[c] = p.op("tensor", lambda e, c=c, sl=sl: e.matmul(
                    p_ex2[:], odiv_sb[:, :], zf2_sb[:, sl:sl + 512],
                    start=(c == 0), stop=(c == DC - 1), skip_group_check=True), inc="pe")
            stats_pe = p.cnt["pe"]
            p.wait("vector", "pe", stats_pe)
            c1 = p.op("vector", lambda e: e.tensor_copy(mr2_sb[:, 0:512], p_mean[:]),
                      inc="ve")
            p.wait("vector", "ve", c1)
            c2 = p.op("vector", lambda e: e.tensor_mul(
                mr2_sb[:, 512:1024], mr2_sb[:, 0:512], mr2_sb[:, 0:512]), inc="ve")
            p.wait("vector", "ve", c2)
            rows_ve = p.op("vector", lambda e: e.tensor_sub(
                mr2_sb[:, 512:1024], p_ex2[:], mr2_sb[:, 512:1024]), inc="ve")
            prev_rows_ve = rows_ve
            p.wait("scalar", "ve", rows_ve)
            l1 = p.op("scalar", lambda e: e.activation(
                mr2_sb[:, 512:1024], mr2_sb[:, 512:1024],
                AF.Ln, bias=eps_sb[0:1, 0:1]), inc="se")
            p.wait("scalar", "se", l1)
            rows_se = p.op("scalar", lambda e: e.activation(
                mr2_sb[:, 512:1024], mr2_sb[:, 512:1024], AF.Exp, scale=-0.5), inc="se")
            p.wait("tensor", "se", rows_se)
            p.op("tensor", lambda e: e.matmul(p_bc[:], orow_sb[:, :],
                                              mr2_sb[0:1, 0:512],
                                              start=True, stop=True))
            bc_pe = p.op("tensor", lambda e: e.matmul(
                p_bc2[:], orow_sb[:, :], mr2_sb[0:1, 512:1024],
                start=True, stop=True), inc="pe")
            zfdone = {}
            for c in range(DC):
                sl = (c % 2) * 512
                if c == 0:
                    p.wait("vector", "pe", bc_pe)
                if c >= 2:
                    pe_v, se_v = zfdone[c - 2]
                    p.wait("vector", "se", se_v)
                s1 = p.op("vector", lambda e, c=c, tc=tc, sl=sl: e.tensor_sub(
                    zf2_sb[:, sl:sl + 512],
                    x_sb[:, c * TPC + tc:c * TPC + tc + 512], p_bc[:]), inc="ve")
                p.wait("vector", "ve", s1)
                zf_ve = p.op("vector", lambda e, sl=sl: e.tensor_mul(
                    zf2_sb[:, sl:sl + 512], zf2_sb[:, sl:sl + 512], p_bc2[:]), inc="ve")
                p.wait("scalar", "ve", zf_ve)
                zc_se = p.op("scalar", lambda e, c=c, tc=tc, sl=sl: e.copy(
                    z2_sb[:, c * TPC + tc:c * TPC + tc + 512], zf2_sb[:, sl:sl + 512]),
                    inc="se")
                zfdone[c] = (0, zc_se)
            zfdone_prev = zfdone
        ln2_se = p.cnt["se"]
        pctx.close()

        p2ctx = ExitStack()
        pq = lambda name, shape: p2ctx.enter_context(
            nc.psum_tensor("aq_" + name, shape, F32))
        p_qk = [pq(f"qk{i}", [128, 512]) for i in range(4)]
        p_qn = pq("qn", [16, 512])
        p_v = [pq(f"v{i}", [128, 512]) for i in range(2)]
        wqv = wq_sb.ap().rearrange("p (s c k) -> p s c k", s=2, c=DC)
        qrv = qraw_sb.ap().rearrange("p (s t) -> p s t", s=2)
        kfv = kf_sb.ap().rearrange("p (s t) -> p s t", s=2)

        gd_wq = {}
        sd_vout = {}
        ve_vt = {}
        se_evac = {}
        pe_rot = {}
        pe_qkv = {}
        ve_rope = {}
        pe_qn = {}
        sd_kout = {}
        ve_max = 0

        def issue_wq(tp, m):
            seq = tp * 16 + m
            if seq >= 2:
                key = (tp, m - 2) if m >= 2 else (tp - 1, 14 + m)
                p.wait("gpsimd", "pe", pe_qkv[key])
            col0 = m * 128
            p.dma("gpsimd", wqv[:, m % 2, :, :],
                  dwqkv[:, col0:col0 + 128].rearrange("(c p) k -> p c k", p=128), "gd")
            gd_wq[(tp, m)] = p.cnt["gd"]

        p.wait("gpsimd", "se", ln2_se)
        for tp in range(2):
            if tp == 0:
                issue_wq(0, 0)
                issue_wq(0, 1)
            for m in range(16):
                seq = tp * 16 + m
                tcs = [tp * 1024, tp * 1024 + 512]
                p.wait("tensor", "gd", gd_wq[(tp, m)])
                if seq == 0:
                    p.wait("tensor", "se", ln2_se)
                if seq >= 2:
                    key = (tp, m - 2) if m >= 2 else (tp - 1, 14 + m)
                    p.wait("tensor", "ve", ve_rope[key])
                for d in range(DC):
                    for j, tc in enumerate(tcs):
                        p.op("tensor", lambda e, m=m, d=d, tc=tc, j=j: e.matmul(
                            p_qk[(m % 2) * 2 + j][:], wqv[:, m % 2, d, :],
                            z2_sb[:, d * TPC + tc:d * TPC + tc + 512],
                            start=(d == 0), stop=(d == DC - 1), skip_group_check=True),
                            inc="pe" if (d == DC - 1 and j == 1) else None)
                pe_qkv[(tp, m)] = p.cnt["pe"]
                issue_wq(tp, m + 2) if m + 2 < 16 else (
                    issue_wq(tp + 1, m + 2 - 16) if tp == 0 else None)

                p.wait("scalar", "pe", pe_qkv[(tp, m)])
                if seq >= 2:
                    key = (tp, m - 2) if m >= 2 else (tp - 1, 14 + m)
                    p.wait("scalar", "ve", ve_rope[key])
                for j in range(2):
                    se_evac[(tp, m)] = p.op("scalar", lambda e, m=m, j=j: e.copy(
                        qrv[:, m % 2, j * 512:(j + 1) * 512], p_qk[(m % 2) * 2 + j][:]),
                        inc="se" if j == 1 else None)
                se_evac[(tp, m)] = p.cnt["se"]

                p.wait("tensor", "se", se_evac[(tp, m)])
                for j in range(2):
                    p.op("tensor", lambda e, m=m, j=j: e.matmul(
                        p_qk[(m % 2) * 2 + j][:], swap_sb[:, :],
                        qrv[:, m % 2, j * 512:(j + 1) * 512],
                        start=True, stop=True, skip_group_check=True),
                        inc="pe" if j == 1 else None)
                pe_rot[(tp, m)] = p.cnt["pe"]

                p.wait("vector", "pe", pe_rot[(tp, m)])
                if m >= 8:
                    key = (tp, m - 2) if m - 2 >= 8 else (
                        (tp - 1, m + 6) if tp >= 1 else None)
                    if key is not None and key in sd_kout:
                        p.wait("vector", "sd", sd_kout[key])
                for j, tc in enumerate(tcs):
                    dst = (q_sb[:, m * TPC + tc:m * TPC + tc + 512] if m < 8
                           else kfv[:, m % 2, j * 512:(j + 1) * 512])
                    p.op("vector", lambda e, m=m, j=j, tc=tc: e.tensor_mul(
                        t1_sb[:, j * 512:(j + 1) * 512],
                        qrv[:, m % 2, j * 512:(j + 1) * 512],
                        cos_sb[:, tc:tc + 512]))
                    t2v = p.op("vector", lambda e, m=m, j=j, tc=tc: e.tensor_mul(
                        t2_sb[:, j * 512:(j + 1) * 512], p_qk[(m % 2) * 2 + j][:],
                        sin_sb[:, tc:tc + 512]), inc="ve")
                    p.wait("vector", "ve", t2v)
                    p.op("vector", lambda e, dst=dst, j=j: e.tensor_add(
                        dst, t1_sb[:, j * 512:(j + 1) * 512],
                        t2_sb[:, j * 512:(j + 1) * 512]),
                        inc="ve" if j == 1 else None)
                ve_rope[(tp, m)] = p.cnt["ve"]

                c = m % 8
                col = 0 if m < 8 else 1
                p.wait("vector", "ve", ve_rope[(tp, m)])
                for j, tc in enumerate(tcs):
                    src = (q_sb[:, m * TPC + tc:m * TPC + tc + 512] if m < 8
                           else kfv[:, m % 2, j * 512:(j + 1) * 512])
                    p.wait("vector", "pe", pe_qn.get((tp, m, j - 1), 0))
                    sq_ve = p.op("vector", lambda e, src=src, j=j: e.tensor_mul(
                        sqa_sb[:, j * 512:(j + 1) * 512], src, src), inc="ve")
                    p.wait("tensor", "ve", sq_ve)
                    if ve_max:
                        p.wait("tensor", "ve", ve_max)
                    pe_qn[(tp, m, j)] = p.op("tensor", lambda e, c=c, j=j: e.matmul(
                        p_qn[:], qnind_sb[:, c * 16:(c + 1) * 16],
                        sqa_sb[:, j * 512:(j + 1) * 512],
                        start=True, stop=True, skip_group_check=True), inc="pe")
                    p.wait("vector", "pe", pe_qn[(tp, m, j)])
                    vv = p.op("vector", lambda e: e.reduce_max(
                        mq_sb[:, 3:4], p_qn[:], axis=AX.X), inc="ve")
                    p.wait("vector", "ve", vv)
                    first = (tp == 0 and m % 8 == 0 and j == 0)
                    if first:
                        ve_max = p.op("vector", lambda e, col=col: e.tensor_copy(
                            mq_sb[:, col:col + 1], mq_sb[:, 3:4]), inc="ve")
                    else:
                        ve_max = p.op("vector", lambda e, col=col: e.tensor_max(
                            mq_sb[:, col:col + 1], mq_sb[:, col:col + 1],
                            mq_sb[:, 3:4]), inc="ve")
                    p.wait("vector", "ve", ve_max)

                if m >= 8:
                    p.wait("sync", "ve", ve_rope[(tp, m)])
                    dst = ap_custom(cc_in, (m - 8) * 128 * TPC + tp * 1024,
                                    [[TPC, 128], [1, 1024]])
                    sd_kout[(tp, m)] = p.dma("sync", dst, kfv[:, m % 2, :], "sd")

            p.wait("tensor", "gd", wv_gd)
            for ts in range(8):
                tok0 = tp * 1024 + ts * 128
                tc8 = tp * 8 + ts
                p.wait("vector", "pe", pe_rot[(tp, 15)])
                if tc8 >= 2:
                    p.wait("vector", "sd", sd_vout[tc8 - 2])
                ms_ve = p.op("vector", lambda e, ts=ts: e.memset(
                    vt_sb[:, (ts % 2) * 1056:(ts % 2) * 1056 + 1056], 1.0), inc="ve")
                if tc8 >= 1:
                    p.wait("tensor", "ve", ve_vt[tc8 - 1])
                for vv in range(2):
                    p.op("tensor", lambda e, tok0=tok0, vv=vv: e.matmul(
                        p_v[vv][:], z2_sb.ap().rearrange(
                            "p (c t) -> p c t", c=DC)[:, 0, tok0:tok0 + 128]
                        if False else z2_sb[:, 0 * TPC + tok0:0 * TPC + tok0 + 128],
                        wv_sb.ap().rearrange("p (c k) -> p c k", c=DC)[:, 0, vv * 512:(vv + 1) * 512],
                        start=True, stop=False, skip_group_check=True))
                    for d in range(1, DC):
                        p.op("tensor", lambda e, tok0=tok0, vv=vv, d=d: e.matmul(
                            p_v[vv][:], z2_sb[:, d * TPC + tok0:d * TPC + tok0 + 128],
                            wv_sb.ap().rearrange("p (c k) -> p c k", c=DC)[:, d, vv * 512:(vv + 1) * 512],
                            start=False, stop=(d == DC - 1), skip_group_check=True),
                            inc="pe" if (d == DC - 1) else None)
                pe_vts = p.cnt["pe"]
                p.wait("vector", "pe", pe_vts)
                p.wait("vector", "ve", ms_ve)
                for vv in range(2):
                    p.op("vector", lambda e, ts=ts, vv=vv: e.tensor_copy(
                        ap_custom(vt_sb, (ts % 2) * 1056 + vv * 8 * VSTR,
                                  [[2 * 1056, 128], [VSTR, 8], [1, 64]]),
                        bass.AP(p_v[vv], 0, [[512, 128], [64, 8], [1, 64]])),
                        inc="ve" if vv == 1 else None)
                ve_vt[tc8] = p.cnt["ve"]
                p.wait("sync", "ve", ve_vt[tc8])
                dst = ap_custom(cc_in, VBASE + tc8 * 128 * 1056,
                                [[1056, 128], [1, 1056]])
                sd_vout[tc8] = p.dma("sync", dst,
                                     vt_sb[:, (ts % 2) * 1056:(ts % 2) * 1056 + 1056],
                                     "sd")

        p.wait("scalar", "ve", ve_max)
        st_se = p.op("scalar", lambda e: e.activation(
            mq_sb[:, 1:2], mq_sb[:, 1:2], AF.Sqrt), inc="se")
        p.wait("scalar", "ve", ve_max)
        st_se = p.op("scalar", lambda e: e.activation(
            mq_sb[:, 2:3], mq_sb[:, 0:1], AF.Sqrt), inc="se")
        p.wait("vector", "se", st_se)
        st_ve = p.op("vector", lambda e: e.tensor_scalar_mul(
            mq_sb[:, 1:2], mq_sb[:, 1:2], 1.004), inc="ve")
        p.wait("gpsimd", "ve", st_ve)
        st_gd = p.dma("gpsimd", ap_custom(cc_in, SBASE, [[1, 16]]), mq_sb[0:16, 1:2],
                      "gd")

        p.wait("gpsimd", "sd", p.cnt["sd"])
        p.wait("gpsimd", "gd", p.cnt["gd"])
        cc1 = p.op("gpsimd", lambda e: e.collective_compute(
            "AllGather", ALU.bypass,
            replica_groups=[[0, 1], [2, 3], [4, 5], [6, 7]],
            ins=[cc_in.ap().opt()], outs=[cc_out.ap().opt()]), inc="cc")
        p2ctx.close()
        pactx.close()
        aend = dict(pe=p.cnt["pe"], ve=p.cnt["ve"], se=p.cnt["se"])

        pbctx = ExitStack()
        bsb = lambda name, shape, dt: pbctx.enter_context(
            nc.sbuf_tensor("ab_" + name, shape, dt))
        mask_sb = bsb("mask", [128, 32 * 512], BF16)
        y_sb = bsb("y", [128, 8 * TPC], BF16)
        kb_sb = bsb("kb", [128, 2 * 8 * 128], BF16)
        vb_sb = bsb("vb", [128, 2 * 1056], BF16)
        a_sb = bsb("aT", [128, 2 * 512], BF16)
        sums_sb = bsb("sums", [16, 512], F32)
        rec_sb = bsb("rec", [16, 512], F32)
        s01_sb = bsb("s01", [16, 2], F32)
        nrow_sb = bsb("nrow", [1, 16], F32)
        headind_sb = bsb("headind", [128, 16 * 16], BF16)
        pairind_sb = bsb("pairind", [16, 8 * 128], F32)
        wp_sb = bsb("wp", [128, 2 * DC * 128], BF16)

        pb = ExitStack()
        pbp = lambda name, shape: pb.enter_context(
            nc.psum_tensor("ab_" + name, shape, F32))
        p_s = [pbp(f"s{i}", [128, 512]) for i in range(2)]
        p_y = [pbp(f"y{i}", [128, 512]) for i in range(4)]
        p_sum = pbp("sum", [16, 512])
        p_nb = pbp("nb", [128, 16])

        for e in ("sync", "gpsimd"):
            p.wait(e, "pe", aend["pe"])
            p.wait(e, "ve", aend["ve"])
            p.wait(e, "se", aend["se"])
        p.dma("sync", mask_sb.ap().rearrange("p (a t) -> p a t", a=32),
              dmask.rearrange("(a p) t -> p a t", p=128), "sd")
        p.dma("sync", headind_sb[:], dheadind[:, :], "sd")
        p.dma("sync", pairind_sb[:], dpairind[:, :], "sd")
        bload_sd = p.cnt["sd"]

        p.wait("gpsimd", "cc", p.cnt["cc"])
        for r in range(2):
            p.dma("gpsimd", s01_sb[0:16, r:r + 1],
                  ap_custom(cc_out, r * CCROWS * TPC + SBASE, [[1, 16]]), "gd")
        st2_gd = p.cnt["gd"]
        p.wait("vector", "gd", st2_gd)

        def vstep2(fn):
            c = p.op("vector", fn, inc="ve")
            p.wait("vector", "ve", c)
            return c
        vstep2(lambda e: e.tensor_max(s01_sb[:, 0:1], s01_sb[:, 0:1], s01_sb[:, 1:2]))
        vstep2(lambda e: e.tensor_mul(s01_sb[:, 1:2], s01_sb[:, 0:1], mq_sb[:, 2:3]))
        nb_ve = vstep2(lambda e: e.tensor_scalar_mul(s01_sb[:, 1:2], s01_sb[:, 1:2],
                                                     -0.1255))
        p.wait("gpsimd", "ve", nb_ve)
        p.dma("gpsimd", dnegb, s01_sb[0:16, 1:2], "gd")
        p.wait("gpsimd", "gd", p.cnt["gd"])
        p.dma("gpsimd", nrow_sb[:], dnegb, "gd")
        nrow_gd = p.cnt["gd"]
        p.wait("tensor", "gd", nrow_gd)
        p.wait("tensor", "cc", p.cnt["cc"])
        p.wait("tensor", "pe", aend["pe"])
        nb_pe = p.op("tensor", lambda e: e.matmul(
            p_nb[:], orow_sb[:, :], nrow_sb[:, :], start=True, stop=True,
            skip_group_check=True), inc="pe")
        p.wait("scalar", "pe", nb_pe)
        p.wait("scalar", "se", aend["se"])
        nb_se = p.op("scalar", lambda e: e.copy(negb_sb[:], p_nb[:]), inc="se")
        if "negb" in taps:
            p.wait("gpsimd", "se", nb_se)
            p.dma("gpsimd", tap_aps["negb"], negb_sb[:], "gd")
        if "q" in taps:
            p.wait("gpsimd", "ve", aend["ve"])
            p.dma("gpsimd", tap_aps["q"].rearrange("(c p) t -> p c t", p=128),
                  q_sb.ap().rearrange("p (c t) -> p c t", c=8), "gd")
        if "kv" in taps:
            p.wait("gpsimd", "cc", p.cnt["cc"])
            p.dma("gpsimd", tap_aps["kv"], cc_out.ap(), "gd")

        seqs = []
        for tau in range(NTAU):
            for sw in range(2):
                for g in range(8 * (tau + 1)):
                    seqs.append((tau, sw, g))
        sd_kv = {}
        pe_av = {}
        se_exp = {}
        ve_msk = {}
        se_yev = {}
        ve_ynm = {}

        def issue_kv(i):
            if i >= len(seqs):
                return
            tau, sw, g = seqs[i]
            sl = i % 2
            if i >= 2:
                p.wait("sync", "pe", pe_av[i - 2])
            else:
                p.wait("sync", "cc", p.cnt["cc"])
                p.wait("sync", "se", aend["se"])
                p.wait("sync", "ve", aend["ve"])
            r, l = owner_and_local(g)
            base = r * CCROWS * TPC
            p.dma("sync", kb_sb.ap().rearrange("p (s c t) -> p s c t", s=2, c=8)
                  [:, sl, :, :],
                  ap_custom(cc_out, base + l * 128, [[TPC, 128], [128 * TPC, 8],
                                                     [1, 128]]), "sd")
            sd_kv[i] = p.dma("sync", vb_sb[:, sl * 1056:(sl + 1) * 1056],
                             ap_custom(cc_out, base + VBASE + l * 128 * 1056,
                                       [[1056, 128], [1, 1056]]), "sd")

        issue_kv(0)
        issue_kv(1)
        first_b = True
        norm_pending = [0]
        ae_ctr = 0
        for i, (tau, sw, g) in enumerate(seqs):
            E = 8 * (tau + 1)
            tc = tau * 512
            for hh in range(8):
                h = sw * 8 + hh
                par = ae_ctr % 2
                ae_ctr += 1
                p.wait("tensor", "sd", sd_kv[i])
                if first_b:
                    p.wait("tensor", "se", nb_se)
                    p.wait("tensor", "ve", aend["ve"])
                    first_b = False
                if ae_ctr >= 3:
                    p.wait("tensor", "se", se_exp[ae_ctr - 2])
                if norm_pending[0]:
                    p.wait("tensor", "ve", norm_pending[0])
                    norm_pending[0] = 0
                sl = i % 2
                p.op("tensor", lambda e, h=h, sl=sl, tc=tc, par=par: e.matmul(
                    p_s[par][:],
                    kb_sb.ap().rearrange("p (s c t) -> p s c t", s=2, c=8)
                    [(h % 2) * 64:(h % 2) * 64 + 64, sl, h // 2, :],
                    q_sb[(h % 2) * 64:(h % 2) * 64 + 64,
                         (h // 2) * TPC + tc:(h // 2) * TPC + tc + 512],
                    start=True, stop=True, skip_group_check=True),
                    inc="pe")
                pe_s = p.cnt["pe"]
                p.wait("scalar", "pe", pe_s)
                if ae_ctr >= 3:
                    p.wait("scalar", "pe", pe_av[("h", ae_ctr - 2)])
                    if ("m", ae_ctr - 2) in ve_msk:
                        pass
                se_exp[ae_ctr] = p.op("scalar", lambda e, par=par, h=h: e.activation(
                    a_sb[:, par * 512:(par + 1) * 512], p_s[par][:], AF.Exp,
                    bias=negb_sb[:, h:h + 1], scale=0.125), inc="se")
                masked = 8 * tau <= g < 8 * tau + 8
                if masked:
                    j = g - 8 * tau
                    p.wait("vector", "se", se_exp[ae_ctr])
                    p.wait("vector", "sd", bload_sd)
                    ve_msk[("m", ae_ctr)] = p.op(
                        "vector", lambda e, par=par, tau=tau, j=j: e.tensor_mul(
                            a_sb[:, par * 512:(par + 1) * 512],
                            a_sb[:, par * 512:(par + 1) * 512],
                            mask_sb[:, (tau * 8 + j) * 512:(tau * 8 + j) * 512 + 512]),
                        inc="ve")
                if masked:
                    p.wait("tensor", "ve", ve_msk[("m", ae_ctr)])
                else:
                    p.wait("tensor", "se", se_exp[ae_ctr])
                if g == 0 and hh <= 1:
                    if (tau, sw) != (0, 0):
                        p.wait("tensor", "se", se_yev[prev_swp])
                pair = hh // 2
                half = hh % 2
                p.op("tensor", lambda e, pair=pair, half=half, sl=sl, par=par, h=h,
                     g=g, E=E: e.matmul(
                         p_y[pair][half * 64:half * 64 + 64, :],
                         vb_sb[:, sl * 1056 + h * VSTR:sl * 1056 + h * VSTR + 64],
                         a_sb[:, par * 512:(par + 1) * 512],
                         start=(g == 0), stop=(g == E - 1 and half == 1),
                         skip_group_check=True))
                squirst = (sw == 0 and g == 0 and hh == 0)
                squast = (sw == 1 and g == E - 1 and hh == 7)
                if squirst and tau > 0:
                    p.wait("tensor", "ve", ve_ynm[tau - 1])
                p.op("tensor", lambda e, h=h, par=par, squirst=squirst,
                     squast=squast: e.matmul(
                         p_sum[:], headind_sb[:, h * 16:(h + 1) * 16],
                         a_sb[:, par * 512:(par + 1) * 512],
                         start=squirst, stop=squast, skip_group_check=True),
                    inc="pe")
                pe_av[("h", ae_ctr)] = p.cnt["pe"]
            pe_av[i] = p.cnt["pe"]
            issue_kv(i + 2)

            if g == E - 1:
                p.wait("scalar", "pe", pe_av[i])
                for pair in range(4):
                    se_yev[(tau, sw)] = p.op(
                        "scalar", lambda e, pair=pair, sw=sw, tau=tau: e.copy(
                            y_sb[:, (sw * 4 + pair) * TPC + tau * 512:
                                 (sw * 4 + pair) * TPC + tau * 512 + 512],
                            p_y[pair][:]), inc="se" if pair == 3 else None)
                se_yev[(tau, sw)] = p.cnt["se"]
                prev_swp = (tau, sw)
                if sw == 1 and not attn_norm:
                    p.wait("vector", "pe", pe_av[i])
                    vstep2(lambda e: e.tensor_copy(sums_sb[:], p_sum[:]))
                    ve_ynm[tau] = p.cnt["ve"]
                    norm_pending[0] = ve_ynm[tau]
                if sw == 1 and attn_norm:
                    p.wait("vector", "pe", pe_av[i])
                    vstep2(lambda e: e.tensor_copy(sums_sb[:], p_sum[:]))
                    vstep2(lambda e: e.tensor_scalar_max(sums_sb[:], sums_sb[:],
                                                         1e-30))
                    rc_ve = vstep2(lambda e: e.reciprocal(rec_sb[:], sums_sb[:]))
                    for pair in range(8):
                        pr_par = pair % 2
                        p.wait("tensor", "ve", rc_ve)
                        p.wait("tensor", "se", se_yev[(tau, 1)])
                        if pair >= 2:
                            p.wait("tensor", "ve", ve_ynm_p[pair - 2])
                        nm_pe = p.op("tensor", lambda e, pair=pair, pr_par=pr_par:
                                     e.matmul(
                                         p_s[pr_par][:],
                                         pairind_sb[:, pair * 128:(pair + 1) * 128],
                                         rec_sb[:, :], start=True, stop=True,
                                         skip_group_check=True), inc="pe")
                        p.wait("vector", "pe", nm_pe)
                        if pair == 0:
                            ve_ynm_p = {}
                        ve_ynm_p[pair] = p.op(
                            "vector", lambda e, pair=pair, tau=tau, pr_par=pr_par:
                            e.tensor_mul(
                                y_sb[:, pair * TPC + tau * 512:
                                     pair * TPC + tau * 512 + 512],
                                y_sb[:, pair * TPC + tau * 512:
                                     pair * TPC + tau * 512 + 512],
                                p_s[pr_par][:]), inc="ve")
                    ve_ynm[tau] = p.cnt["ve"]
                    norm_pending[0] = ve_ynm[tau]
                    ae_ctr += 2
                    se_exp[ae_ctr] = p.cnt["se"]
                    se_exp[ae_ctr - 1] = p.cnt["se"]
                    pe_av[("h", ae_ctr)] = p.cnt["pe"]
                    pe_av[("h", ae_ctr - 1)] = p.cnt["pe"]

        gd_wp = {}

        def issue_wp(dcx):
            if dcx >= DC:
                return
            if dcx >= 2:
                p.wait("gpsimd", "pe", pe_proj[dcx - 2])
            else:
                p.wait("gpsimd", "cc", p.cnt["cc"])
            p.dma("gpsimd", wp_sb.ap().rearrange("p (s c k) -> p s c k", s=2, c=DC)
                  [:, dcx % 2, :, :],
                  dwproj[:, dcx * 128:(dcx + 1) * 128].rearrange(
                      "(c p) k -> p c k", p=128), "gd")
            gd_wp[dcx] = p.cnt["gd"]

        pe_proj = {}
        ve_pr = {}
        issue_wp(0)
        issue_wp(1)
        wpv = wp_sb.ap().rearrange("p (s c k) -> p s c k", s=2, c=DC)
        for dcx in range(DC):
            p.wait("tensor", "gd", gd_wp[dcx])
            p.wait("tensor", "ve", ve_ynm[NTAU - 1])
            if dcx >= 1:
                p.wait("tensor", "ve", ve_pr[dcx - 1])
            for c in range(DC):
                for tau in range(NTAU):
                    p.op("tensor", lambda e, dcx=dcx, c=c, tau=tau: e.matmul(
                        p_y[tau][:], wpv[:, dcx % 2, c, :],
                        y_sb[:, c * TPC + tau * 512:c * TPC + tau * 512 + 512],
                        start=(c == 0), stop=(c == DC - 1), skip_group_check=True),
                        inc="pe" if (c == DC - 1 and tau == NTAU - 1) else None)
            pe_proj[dcx] = p.cnt["pe"]
            issue_wp(dcx + 2)
            p.wait("vector", "pe", pe_proj[dcx])
            for tau in range(NTAU):
                p.op("vector", lambda e, dcx=dcx, tau=tau: e.tensor_add(
                    x_sb[:, dcx * TPC + tau * 512:dcx * TPC + tau * 512 + 512],
                    x_sb[:, dcx * TPC + tau * 512:dcx * TPC + tau * 512 + 512],
                    p_y[tau][:]), inc="ve" if tau == NTAU - 1 else None)
            ve_pr[dcx] = p.cnt["ve"]

        if "sums" in taps:
            p.wait("gpsimd", "ve", p.cnt["ve"])
            p.dma("gpsimd", tap_aps["sums"], sums_sb[:], "gd")
        if "aT" in taps:
            p.wait("gpsimd", "ve", p.cnt["ve"])
            p.wait("gpsimd", "se", p.cnt["se"])
            p.dma("gpsimd", tap_aps["aT"], a_sb[:], "gd")
        if "y" in taps:
            p.wait("gpsimd", "ve", p.cnt["ve"])
            p.dma("gpsimd", tap_aps["y"], y_sb[:], "gd")
        pb.close()
        pbctx.close()
        actx.close()
        return dict(pe=p.cnt["pe"], ve=p.cnt["ve"], se=p.cnt["se"],
                    sd=p.cnt["sd"], gd=p.cnt["gd"])

    if "ffn1" in phases:
        entry = emit_ffn("f1_", tap_z=tap_aps.get("z1"), tap_gate=tap_aps.get("gate1"),
                         tap_x=tap_aps.get("x1"))
    if "attn" in phases:
        entry = emit_attn(entry)
        if "x2" in taps:
            p.wait("sync", "ve", entry["ve"])
            p.dma("sync", tap_aps["x2"].rearrange("(c p) t -> p c t", p=128),
                  x_sb.ap().rearrange("p (c t) -> p c t", c=DC), "sd")
    if "ffn2" in phases:
        entry = emit_ffn("f2_", tap_x=tap_aps.get("x2"))

    p.wait("sync", "ve", p.cnt["ve"])
    p.wait("sync", "se", p.cnt["se"])
    p.wait("sync", "pe", p.cnt["pe"])
    p.dma("sync", dout.rearrange("(c p) t -> p c t", p=128),
          x_sb.ap().rearrange("p (c t) -> p c t", c=DC), "sd")
    final_sd = p.cnt["sd"]
    p.wait("sync", "sd", final_sd)

    blk = ctx.enter_context(nc.Block())
    p.replay(blk)
    ctx.close()
    return nc



def make_in_maps(inputs, taps=()):
    tbl = host_tables()
    w = host_weights(inputs)
    x = np.asarray(inputs["L0_attn_x"])
    in_maps = []
    for core in range(8):
        x_fm, cos_fm, sin_fm, masks = host_percore(x, core)
        m = {"x_fm": x_fm, "cos_fm": cos_fm, "sin_fm": sin_fm, "masks": masks}
        m.update({k: v for k, v in w.items()
                  if k in ("wqkv", "wproj", "w1", "w2", "w3", "b1c", "b2c", "b3",
                           "gatew", "gateb")})
        m.update({"swapneg": tbl["swapneg"], "ident_f32": tbl["ident_f32"],
                  "ones_div": tbl["ones_div"], "ones_row": tbl["ones_row"],
                  "exp_ind": tbl["exp_ind"], "head_ind": tbl["head_ind"],
                  "qn_ind": tbl["qn_ind"], "pair_ind": tbl["pair_ind"]})
        in_maps.append(m)
    return in_maps


def assemble_output(results, key="out_fm"):
    out = np.zeros((B, T, D), dtype=np.float32)
    for core in range(8):
        rank, batch = core & 1, core >> 1
        toks = tokens_for_rank(rank)
        out[batch].reshape(T, D)[toks] = results[core][key].T
    return out


def kernel(**inputs):
    inputs = {k: np.asarray(v) for k, v in inputs.items()}
    nc = build_program(phases=DEV_PHASES, nexp=DEV_NEXP, taps=DEV_TAPS)
    in_maps = make_in_maps(inputs, taps=DEV_TAPS)
    res = run_bass_kernel_spmd(nc, in_maps, core_ids=list(range(8)))
    return assemble_output(res.results)
